# version 09
# speedup vs baseline: 1.4735x; 1.0267x over previous
import math
import numpy as np
import ml_dtypes
import concourse.bass as bass
import concourse.mybir as mybir
from concourse.bass_utils import run_bass_kernel_spmd

F32 = mybir.dt.float32
BF16 = mybir.dt.bfloat16
AF = mybir.ActivationFunctionType
ALU = mybir.AluOpType
AX = mybir.AxisListType

B = 8
T = 4096
D = 1024
H = 16
DH = 64
M = 256
HID = 4096
NCH = T // 128
NSUP = T // 512
EPS_K = 1e-4
LN_EPS = 1e-5
DNORM = float(DH ** (-0.25))
NDIAG = -0.5 * float(DH ** (-0.5))
MAGIC = float(np.float32(1.5 * 2 ** 23))

SEM_DMA = 16


class Sem:
    def __init__(self, nc, name):
        self.h = nc.alloc_semaphore(name)
        self.n = 0

    def inc(self, k=1):
        self.n += k
        return self.n


class Sched:
    ENGINES = ("sync", "vector", "scalar", "tensor", "gpsimd")

    def __init__(self, nc):
        self.nc = nc
        self.items = {e: [] for e in self.ENGINES}

    def on(self, engine, fn, waits=()):
        self.items[engine].append((list(waits), fn))

    def build(self):
        nc = self.nc
        with nc.Block() as block:
            for e in self.ENGINES:
                items = self.items[e]
                if not items:
                    continue

                def body(eng, items=items):
                    for waits, fn in items:
                        for (s, v) in waits:
                            eng.wait_ge(s.h, v)
                        fn(eng)

                getattr(block, e)(body)


def _bcast_col(ap_col, n):
    return ap_col.broadcast_to((ap_col.shape[0], 1, n))


def build_program(mode):
    nc = bass.Bass("TRN2", target_bir_lowering=False, debug=False)
    S = Sched(nc)

    x_d = nc.dram_tensor("x", [T, D], F32, kind="ExternalInput").ap()
    wk_d = nc.dram_tensor("wk", [D, D], F32, kind="ExternalInput").ap()
    projT_d = nc.dram_tensor("projT", [DH, M], F32, kind="ExternalInput").ap()
    identbf_d = nc.dram_tensor("identbf", [128, 128], BF16, kind="ExternalInput").ap()
    oblk_d = nc.dram_tensor("oblk", [128, 136], BF16, kind="ExternalInput").ap()
    if mode == "main":
        wq_d = nc.dram_tensor("wq", [D, D], F32, kind="ExternalInput").ap()
        wv_d = nc.dram_tensor("wv", [D, D], F32, kind="ExternalInput").ap()
        wo_d = nc.dram_tensor("wo", [D, D], F32, kind="ExternalInput").ap()
        bo_d = nc.dram_tensor("bo", [1, D], F32, kind="ExternalInput").ap()
        w1s_d = nc.dram_tensor("w1s", [HID // 128, D, 128], F32, kind="ExternalInput").ap()
        w2t_d = nc.dram_tensor("w2t", [HID, D], F32, kind="ExternalInput").ap()
        cw_d = nc.dram_tensor("cw", [128, 3 * (HID // 128)], F32, kind="ExternalInput").ap()
        cb_d = nc.dram_tensor("cb", [128, HID // 128], F32, kind="ExternalInput").ap()
        onesrow_d = nc.dram_tensor("onesrow", [1, 128], BF16, kind="ExternalInput").ap()
        stab8_d = nc.dram_tensor("stab8", [1, 8], F32, kind="ExternalInput").ap()
        y_d = nc.dram_tensor("y", [T, D], F32, kind="ExternalOutput").ap()
        xnd = nc.dram_tensor("xnd", [8, 128, T], BF16, kind="Internal").ap()
        xqd = nc.dram_tensor("xqd", [8, 128, T], BF16, kind="Internal").ap()
        xattnd = nc.dram_tensor("xattnd", [T, D], F32, kind="Internal").ap()
        h2d = nc.dram_tensor("h2d", [HID // 128, 128, T], BF16, kind="Internal").ap()
        w2qd = nc.dram_tensor("w2qd", [HID // 128, 128, D], BF16, kind="Internal").ap()
        rowd = nc.dram_tensor("rowd", [16, T], F32, kind="Internal").ap()
    else:
        ystab_d = nc.dram_tensor("ystab", [1, 1], F32, kind="ExternalOutput").ap()
        xnd = nc.dram_tensor("xnd", [8, 128, T], BF16, kind="Internal").ap()

    sb_at = nc.alloc_sbuf_tensor_at
    OFF = 16512
    OFFSETS = {}

    def place(name, shape, dtype, align=64):
        nonlocal OFF
        OFF = (OFF + align - 1) // align * align
        t = sb_at(name, shape, dtype, offset=OFF)
        OFFSETS[name] = OFF
        bytes_per_part = int(np.prod(shape[1:])) * mybir.dt.size(dtype)
        OFF += bytes_per_part
        return t

    def overlay0(name, shape, dtype, base_name, delta=0):
        return sb_at(name, shape, dtype, offset=OFFSETS[base_name] + delta)

    identbf = place("identbf", [128, 128], BF16)
    oblk = place("oblk", [128, 136], BF16)
    paug = place("paug", [128, 2 * M], BF16)
    projsb = place("projsb", [DH, M], F32)
    colsT = place("cols", [128, 48], F32)
    statc = place("statc", [128, 128], F32)
    xbuf = [place(f"xbuf{i}", [128, D], F32) for i in range(2)]
    xn_tm = place("xn_tm", [128, D], BF16)
    u_tm = place("u_tm", [128, D], F32)
    xq_tm = place("xq_tm", [128, D], BF16)
    sq_scr = place("sq_scr", [128, D], BF16)
    xnf = place("xnf", [128, 8 * 512], BF16)
    xqf = place("xqf", [128, 8 * 512], BF16)
    kpair = place("kpair", [128, 2 * T + 8], BF16)
    xsupb = [overlay0("xsup0", [128, 4 * D], F32, "kpair")]
    xnsupb = [overlay0("xnsup", [128, 8 * 512], BF16, "xn_tm"),
              overlay0("xnsup2", [128, 8 * 512], BF16, "xqf")]
    vaug = place("vaug", [128, NCH * 1040], BF16)
    eaug = [place(f"eaug{i}", [128, 4 * 257], BF16) for i in range(2)]
    small = place("small", [128, 64], F32)
    stabcol = place("stabcol", [128, 2], F32)
    maxcol = place("maxcol", [128, 1], F32)
    negd_sb = place("negd_sb", [32, 512], F32)

    pb = [nc.alloc_psum_tensor(f"pb{i}", [128, 512], F32) for i in range(8)]

    s_ld = Sem(nc, "s_ld")
    s_ldwk = Sem(nc, "s_ldwk")
    s_id32 = Sem(nc, "s_id32")
    s_paug = Sem(nc, "s_paug")
    s_xp = [Sem(nc, "s_xa"), Sem(nc, "s_xb")]

    s_gp = Sem(nc, "s_gp")
    s_s2 = Sem(nc, "s_s2")
    s_cols = Sem(nc, "s_cols")
    s_xn = Sem(nc, "s_xn")
    s_u = Sem(nc, "s_u")
    s_xq = Sem(nc, "s_xq")
    s_tp = Sem(nc, "s_tp")
    s_ev = Sem(nc, "s_ev")
    s_spill = Sem(nc, "s_spill")
    s_vmm = Sem(nc, "s_vmm")
    s_vev = Sem(nc, "s_vev")
    s_xnsp = [Sem(nc, "s_xnsa"), Sem(nc, "s_xnsb")]
    s_kmm = Sem(nc, "s_kmm")
    s_kev = Sem(nc, "s_kev")
    s_ksq = Sem(nc, "s_ksq")
    s_dkmm = Sem(nc, "s_dkmm")
    s_dksub = Sem(nc, "s_dksub")
    s_exp = Sem(nc, "s_exp")
    s_red = Sem(nc, "s_red")
    s_fin = Sem(nc, "s_fin")

    cA = float(DNORM)

    def prologue():
        def f(g):
            g.dma_start(out=identbf[:], in_=identbf_d[:, :]).then_inc(s_ld.h, 16)
            g.dma_start(out=oblk[:], in_=oblk_d[:, :]).then_inc(s_ld.h, 16)
            g.dma_start(out=projsb[:], in_=projT_d[:, :]).then_inc(s_ld.h, 16)
        S.on("gpsimd", f)
        s_ld.inc(3 * 16)
        ld_target = s_ld.n

        def f2(v):
            v.memset(paug[:], 0.0)
            v.memset(colsT[:, 22:23], MAGIC)
            v.memset(colsT[:, 46:47], MAGIC)
            v.drain()
            v.tensor_scalar_mul(paug[0:DH, 0:M], projsb[:, :], cA)
            v.tensor_scalar_mul(paug[DH:128, M:2 * M], projsb[:, :], cA).then_inc(s_paug.h, 1)
        S.on("vector", f2, waits=[(s_ld, ld_target)])
        s_paug.inc(1)
        return s_paug.n

    wkbf = place("wkbf", [128, 8 * 1024], BF16)

    def load_wk():
        def f(g):
            for dinc in range(8):
                g.dma_start(out=wkbf[:, dinc * 1024:(dinc + 1) * 1024],
                            in_=wk_d[dinc * 128:(dinc + 1) * 128, :]).then_inc(s_ldwk.h, 16)
        S.on("gpsimd", f)
        s_ldwk.inc(8 * 16)
        return s_ldwk.n

    def ln_chunk(c, with_xq):
        buf = xbuf[c % 2]
        cols = colsT[:, (c % 2) * 24:(c % 2) * 24 + 24]
        if c >= 2:
            w = [(s_gp, c - 1), (s_s2, 2 * (c - 1)), (s_xn, c - 1)]
            if with_xq:
                w.append((s_u, c - 1))
        else:
            w = []

        sx = s_xp[c % 2]

        def f_dma(sy, c=c, buf=buf, sx=sx):
            sy.dma_start(out=buf[:], in_=x_d[c * 128:(c + 1) * 128, :]).then_inc(sx.h, 16)
        S.on("sync", f_dma, waits=w)
        sx.inc(16)
        x_rdy = sx.n

        def f_gp(v, buf=buf, cols=cols):
            v.reduce_sum(cols[:, 0:1], buf[:], AX.X)
            v.reduce_max(cols[:, 1:2], buf[:], AX.X)
            v.tensor_reduce(cols[:, 2:3], buf[:], AX.X, ALU.min).then_inc(s_gp.h, 1)
        S.on("vector", f_gp, waits=[(sx, x_rdy), (s_cols, max(0, s_cols.n - 2))])
        s_gp.inc(1)

        def f_s2(sc, buf=buf, cols=cols):
            sc.activation(out=sq_scr[:], in_=buf[:], func=AF.Square,
                          accum_out=cols[:, 3:4]).then_inc(s_s2.h, 1)
        S.on("scalar", f_s2, waits=[(sx, x_rdy), (s_cols, max(0, s_cols.n - 2))])
        s_s2.inc(1)

        def f_cols(v, c=c, cols=cols):
            v.tensor_scalar_mul(cols[:, 4:5], cols[:, 0:1], 1.0 / D)
            v.drain()
            v.tensor_scalar_mul(cols[:, 5:6], cols[:, 3:4], 1.0 / D)
            v.tensor_mul(cols[:, 6:7], cols[:, 4:5], cols[:, 4:5])
            v.drain()
            v.tensor_sub(cols[:, 7:8], cols[:, 5:6], cols[:, 6:7])
            v.drain()
            v.tensor_scalar_add(cols[:, 7:8], cols[:, 7:8], LN_EPS).then_inc(s_cols.h, 1)
        S.on("vector", f_cols, waits=[(s_gp, s_gp.n), (s_s2, s_s2.n)])
        c1 = s_cols.inc(1)

        def f_sqrt(sc, cols=cols):
            sc.activation(out=cols[:, 8:9], in_=cols[:, 7:8],
                          func=AF.Sqrt).then_inc(s_s2.h, 1)
        S.on("scalar", f_sqrt, waits=[(s_cols, c1)])
        c2 = s_s2.inc(1)

        def f_cols2(v, c=c, with_xq=with_xq, cols=cols):
            if with_xq:
                v.tensor_sub(cols[:, 11:12], cols[:, 1:2], cols[:, 4:5])
                v.tensor_sub(cols[:, 12:13], cols[:, 4:5], cols[:, 2:3])
            v.reciprocal(cols[:, 9:10], cols[:, 8:9])
            v.drain()
            nb = v.scalar_tensor_tensor(cols[:, 10:11], cols[:, 4:5], -1.0, cols[:, 9:10],
                                        ALU.mult, ALU.mult)
            if with_xq:
                v.scalar_tensor_tensor(cols[:, 21:22], cols[:, 4:5], -127.0, cols[:, 9:10],
                                       ALU.mult, ALU.mult)
                v.tensor_max(cols[:, 13:14], cols[:, 11:12], cols[:, 12:13])
                v.drain()
                v.tensor_scalar(cols[:, 15:16], cols[:, 13:14], cols[:, 9:10], 1e-5,
                                ALU.mult, ALU.max)
                v.drain()
                v.tensor_scalar_mul(cols[:, 20:21], cols[:, 15:16], 1.0 / 127.0)
                v.reciprocal(cols[:, 16:17], cols[:, 15:16])
                v.drain()
                v.tensor_scalar(cols[:, 17:18], cols[:, 16:17], cols[:, 9:10], 127.0,
                                ALU.mult, ALU.mult)
                v.scalar_tensor_tensor(cols[:, 19:20], cols[:, 16:17], cols[:, 21:22],
                                       cols[:, 22:23], ALU.mult, ALU.add)
                v.tensor_copy(statc[:, c:c + 1], cols[:, 20:21]).then_inc(s_cols.h, 1)
            else:
                nb.then_inc(s_cols.h, 1)
        S.on("vector", f_cols2, waits=[(s_s2, c2)])
        c3 = s_cols.inc(1)

        def f_xn(sc, buf=buf, cols=cols):
            sc.activation(out=xn_tm[:], in_=buf[:], func=AF.Identity,
                          bias=cols[:, 10:11], scale=cols[:, 9:10]).then_inc(s_xn.h, 1)
        S.on("scalar", f_xn, waits=[(s_cols, c3), (s_tp, s_tp.n)])
        s_xn.inc(1)

        if with_xq:
            def f_u(sc, buf=buf, cols=cols):
                sc.activation(out=u_tm[:], in_=buf[:], func=AF.Identity,
                              bias=cols[:, 19:20], scale=cols[:, 17:18]).then_inc(s_u.h, 1)
            S.on("scalar", f_u, waits=[(s_cols, c3), (s_xq, s_xq.n)])
            s_u.inc(1)

            def f_xq(v, cols=cols):
                v.tensor_scalar(xq_tm[:], u_tm[:], MAGIC, cols[:, 20:21],
                                ALU.subtract, ALU.mult).then_inc(s_xq.h, 1)
            S.on("vector", f_xq, waits=[(s_u, s_u.n), (s_tp, s_tp.n)])
            s_xq.inc(1)

    def transpose_chunk(c, with_xq, paug_rdy):
        ci = c % 4
        psA = pb[0 if c % 2 == 0 else 4].ap().bitcast(BF16)
        psB = pb[1 if c % 2 == 0 else 5].ap().bitcast(BF16)

        def f_tp(t, c=c):
            last = None
            for j in range(8):
                last = nc.tensor.transpose(psA[:, j * 128:(j + 1) * 128],
                                           xn_tm[:, j * 128:(j + 1) * 128], identbf[:])
            if with_xq:
                for j in range(8):
                    last = nc.tensor.transpose(psB[:, j * 128:(j + 1) * 128],
                                               xq_tm[:, j * 128:(j + 1) * 128], identbf[:])
            last.then_inc(s_tp.h, 1)
        waits = [(s_xn, s_xn.n), (s_paug, paug_rdy), (s_ev, max(0, s_ev.n - 1))]
        if with_xq:
            waits.append((s_xq, s_xq.n))
        S.on("tensor", f_tp, waits=waits)
        s_tp.inc(1)

        def f_ev(v, ci=ci, with_xq=with_xq):
            src = psA.rearrange("p (j t) -> p j t", j=8)
            dst = bass.AP(xnf.ap().tensor, ci * 128,
                          [[8 * 512, 128], [512, 8], [1, 128]])
            last = v.tensor_copy(dst, src)
            if with_xq:
                srcq = psB.rearrange("p (j t) -> p j t", j=8)
                dstq = bass.AP(xqf.ap().tensor, ci * 128,
                               [[8 * 512, 128], [512, 8], [1, 128]])
                last = v.tensor_copy(dstq, srcq)
            last.then_inc(s_ev.h, 1)
        S.on("vector", f_ev, waits=[(s_tp, s_tp.n), (s_kmm, s_kmm.n),
                                    (s_spill, s_spill.n), (s_vmm, s_vmm.n)])
        s_ev.inc(1)

    def spill_super(s, with_xq):
        def f(sy, s=s):
            for j in range(8):
                sy.dma_start(out=xnd[j, :, s * 512:(s + 1) * 512],
                             in_=xnf[:, j * 512:(j + 1) * 512]).then_inc(s_spill.h, 16)
            if with_xq:
                for j in range(8):
                    sy.dma_start(out=xqd[j, :, s * 512:(s + 1) * 512],
                                 in_=xqf[:, j * 512:(j + 1) * 512]).then_inc(s_spill.h, 16)
        S.on("sync", f, waits=[(s_ev, s_ev.n)])
        s_spill.inc((16 if with_xq else 8) * 16)

    def kproj_pair(d0, wk_rdy):
        for s in range(NSUP):
            xnsup = xnsupb[s % 2]

            sxn = s_xnsp[s % 2]

            def f_ld(sy, s=s, xnsup=xnsup, sxn=sxn):
                for j in range(8):
                    sy.dma_start(out=xnsup[:, j * 512:(j + 1) * 512],
                                 in_=xnd[j, :, s * 512:(s + 1) * 512]).then_inc(sxn.h, 16)
            S.on("sync", f_ld, waits=[(s_kmm, max(0, s_kmm.n - 1)), (s_tp, s_tp.n),
                                      (s_xq, s_xq.n), (s_spill, s_spill.n)])
            sxn.inc(8 * 16)

            def f_mm(t, s=s, d0=d0, xnsup=xnsup):
                mm = None
                for dd in range(2):
                    dout = d0 + dd
                    ps = pb[2 + dd]
                    for din in range(8):
                        mm = nc.tensor.matmul(
                            ps[:, :],
                            wkbf[:, din * 1024 + dout * 128: din * 1024 + (dout + 1) * 128],
                            xnsup[:, din * 512:(din + 1) * 512],
                            start=(din == 0), stop=(din == 7))
                mm.then_inc(s_kmm.h, 1)
            S.on("tensor", f_mm, waits=[(sxn, sxn.n), (s_ldwk, wk_rdy), (s_kev, s_kev.n),
                                        (s_ctxe, s_ctxe.n)])
            s_kmm.inc(1)

            def f_ev(v, s=s):
                last = None
                for dd in range(2):
                    last = v.tensor_copy(kpair[:, dd * T + s * 512: dd * T + (s + 1) * 512],
                                         pb[2 + dd][:, :])
                last.then_inc(s_kev.h, 1)
            S.on("vector", f_ev, waits=[(s_kmm, s_kmm.n)])
            s_kev.inc(1)

    if mode == "stab":
        paug_rdy = prologue()
        wk_rdy = load_wk()

        def f_init(v):
            v.memset(maxcol[:], -1e30).then_inc(s_red.h, 1)
        S.on("vector", f_init)
        s_red.inc(1)

        for c in range(NCH):
            ln_chunk(c, with_xq=False)
            transpose_chunk(c, with_xq=False, paug_rdy=paug_rdy)
            if c % 4 == 3:
                s_sup = c // 4
                def f_kmm(t, kev_base=s_kev.n):
                    for dout in range(8):
                        if dout >= 2:
                            t.wait_ge(s_kev.h, kev_base + dout - 1)
                        ps = pb[2 + dout % 2]
                        for din in range(8):
                            mm = nc.tensor.matmul(
                                ps[:, :],
                                wkbf[:, din * 1024 + dout * 128: din * 1024 + (dout + 1) * 128],
                                xnf[:, din * 512:(din + 1) * 512],
                                start=(din == 0), stop=(din == 7))
                        mm.then_inc(s_kmm.h, 1)
                S.on("tensor", f_kmm,
                     waits=[(s_ev, s_ev.n), (s_ldwk, wk_rdy), (s_kev, s_kev.n), (s_dkmm, s_dkmm.n)])
                kmm_base = s_kmm.n
                s_kmm.inc(8)

                def f_kev(v, kmm_base=kmm_base):
                    for dout in range(8):
                        v.wait_ge(s_kmm.h, kmm_base + dout + 1)
                        v.tensor_copy(kpair[:, dout * 512:(dout + 1) * 512],
                                      pb[2 + dout % 2][:, :]).then_inc(s_kev.h, 1)
                S.on("vector", f_kev)
                s_kev.inc(8)

                for ci in range(4):
                    for g in range(2):
                        def f_dk(t, ci=ci, g=g):
                            for bm in range(4):
                                pair = g * 4 + bm
                                ps = pb[4 + bm]
                                mm = nc.tensor.matmul(
                                    ps[:, :],
                                    kpair[:, pair * 512 + ci * 128: pair * 512 + (ci + 1) * 128],
                                    paug[:, :],
                                    start=True, stop=True)
                            mm.then_inc(s_dkmm.h, 1)
                        S.on("tensor", f_dk, waits=[(s_kev, s_kev.n), (s_red, s_red.n)])
                        s_dkmm.inc(1)

                        def f_red(v):
                            for bm in range(4):
                                v.reduce_max(small[:, 20:21], pb[4 + bm][:, :], AX.X)
                                v.drain()
                                v.tensor_max(maxcol[:], maxcol[:], small[:, 20:21])
                                v.drain()
                            v.nop().then_inc(s_red.h, 1)
                        S.on("vector", f_red, waits=[(s_dkmm, s_dkmm.n)])
                        s_red.inc(1)

        ident32 = place("ident32", [128, 128], F32)

        def f_ld32(g):
            g.dma_start(out=ident32[:], in_=identbf_d[:, :]).then_inc(s_id32.h, 16)
        S.on("gpsimd", f_ld32)
        id32_rdy = s_id32.inc(16)

        def f_tr2(t):
            nc.tensor.transpose(pb[0][0:1, 0:128], maxcol[:, 0:1], ident32[:, :]).then_inc(s_tp.h, 1)
        S.on("tensor", f_tr2, waits=[(s_red, s_red.n), (s_id32, id32_rdy)])
        s_tp.inc(1)

        def f_fin(v):
            v.reduce_max(small[0:1, 0:1], pb[0][0:1, 0:128], AX.X).then_inc(s_red.h, 1)
        S.on("vector", f_fin, waits=[(s_tp, s_tp.n)])
        s_red.inc(1)

        def f_st(sy):
            sy.dma_start(out=ystab_d[:, :], in_=small[0:1, 0:1]).then_inc(s_fin.h, 16)
            sy.wait_ge(s_fin.h, 16)
        S.on("sync", f_st, waits=[(s_red, s_red.n)])
        s_fin.inc(16)

        S.build()
        return nc


    assert mode == "main"
    paug_rdy = prologue()
    wk_rdy = load_wk()

    s_ldwv = Sem(nc, "s_ldwv")
    s_ldwq = Sem(nc, "s_ldwq")
    s_ldwo = Sem(nc, "s_ldwo")
    s_ldm = Sem(nc, "s_ldm")
    s_stab = Sem(nc, "s_stab")
    s_stabb = Sem(nc, "s_stabb")
    s_vset = Sem(nc, "s_vset")
    s_dg = Sem(nc, "s_dg")
    s_dge = Sem(nc, "s_dge")
    s_dks = Sem(nc, "s_dks")
    s_ctx = Sem(nc, "s_ctx")
    s_ctxe = Sem(nc, "s_ctxe")
    s_rhst = Sem(nc, "s_rhst")
    s_qmm = Sem(nc, "s_qmm")
    s_qev = Sem(nc, "s_qev")
    s_dqmm = Sem(nc, "s_dqmm")
    s_dqsub = Sem(nc, "s_dqsub")
    s_eq = Sem(nc, "s_eq")
    s_eqt = Sem(nc, "s_eqt")
    s_eqe = Sem(nc, "s_eqe")
    s_amm = Sem(nc, "s_amm")
    s_asm = Sem(nc, "s_asm")
    s_att = Sem(nc, "s_att")
    s_ate = Sem(nc, "s_ate")
    s_womm = Sem(nc, "s_womm")
    s_xa = Sem(nc, "s_xa2")
    s_xb = Sem(nc, "s_xb2")
    s_xae = Sem(nc, "s_xae")
    s_xad = Sem(nc, "s_xad")
    s_w1m = Sem(nc, "s_w1m")
    s_w1a = Sem(nc, "s_w1a")
    s_wmean = Sem(nc, "s_wmean")
    s_wmb = Sem(nc, "s_wmb")
    s_wst = Sem(nc, "s_wst")
    s_wst2 = Sem(nc, "s_wst2")
    s_wq1 = Sem(nc, "s_wq1")
    s_wq2 = Sem(nc, "s_wq2")
    s_wq2g = Sem(nc, "s_wq2g")
    s_xqr = Sem(nc, "s_xqr")
    s_h1 = Sem(nc, "s_h1")
    s_h1e = Sem(nc, "s_h1e")
    s_cv1 = Sem(nc, "s_cv1")
    s_cv2 = Sem(nc, "s_cv2")
    s_cv3 = Sem(nc, "s_cv3")
    s_sil = Sem(nc, "s_sil")
    s_mx = Sem(nc, "s_mx")
    s_h2d = Sem(nc, "s_h2d")
    s_w2dp = [Sem(nc, "s_w2da"), Sem(nc, "s_w2db")]
    s_mxt = Sem(nc, "s_mxt")
    s_mxr = Sem(nc, "s_mxr")
    s_s2d = Sem(nc, "s_s2d")
    s_h2rp = [Sem(nc, "s_h2ra"), Sem(nc, "s_h2rb")]
    s_w2r = Sem(nc, "s_w2r")
    s_qa = Sem(nc, "s_qa")
    s_qb = Sem(nc, "s_qb")
    s_omm = Sem(nc, "s_omm")
    s_xat2 = Sem(nc, "s_xat2")
    s_oev = Sem(nc, "s_oev")
    s_out = Sem(nc, "s_out")

    wvbf = place("wvbf", [128, 8 * 1024], BF16)
    diagks = place("diagks", [128, 4 * NCH], F32)
    ctx_sb = place("ctx_sb", [65, H * 257], F32)
    xsupb.append(overlay0("xsup1", [128, 4 * D], F32, "ctx_sb"))
    rhs_sb = place("rhs_sb", [128, H * 2 * 66], BF16)
    out_sb = place("out_sb", [128, D], F32)
    onesrow_sb = place("onesrow_sb", [1, 128], BF16)
    bo_sb = place("bo_sb", [1, D], BF16)

    def load_main_weights():
        def f(g):
            for dinc in range(8):
                g.dma_start(out=wvbf[:, dinc * 1024:(dinc + 1) * 1024],
                            in_=wv_d[dinc * 128:(dinc + 1) * 128, :]).then_inc(s_ldwv.h, 16)
        S.on("gpsimd", f)
        s_ldwv.inc(8 * 16)

        def f2(g):
            g.dma_start(out=onesrow_sb[:], in_=onesrow_d[:, :]).then_inc(s_ldm.h, 16)
            g.dma_start(out=bo_sb[:], in_=bo_d[:, :]).then_inc(s_ldm.h, 16)
            g.dma_start(out=small[0:1, 32:40], in_=stab8_d[:, :]).then_inc(s_ldm.h, 16)
        S.on("gpsimd", f2)
        s_ldm.inc(3 * 16)
        return s_ldwv.n, s_ldm.n

    wv_rdy, ldm_rdy = load_main_weights()

    def f_stab(v):
        v.reduce_max(small[0:1, 41:42], small[0:1, 32:40], AX.X).then_inc(s_stab.h, 1)
    S.on("vector", f_stab, waits=[(s_ldm, ldm_rdy)])
    s_stab.inc(1)

    def f_stabd(sy):
        sy.dma_start(out=rowd[15, 0:1], in_=small[0:1, 41:42]).then_inc(s_stabb.h, 16)
        sy.wait_ge(s_stabb.h, 16)
        src = bass.AP(rowd.tensor, 15 * T, [[0, 128], [1, 1]])
        sy.dma_start(out=stabcol[:, 0:1], in_=src).then_inc(s_stabb.h, 16)
    S.on("sync", f_stabd, waits=[(s_stab, s_stab.n)])
    s_stabb.inc(32)
    stab_rdy = s_stabb.n

    def f_ones(v):
        ones_ap = bass.AP(vaug.ap().tensor, 64, [[NCH * 1040, 128], [1040, NCH], [65, 16]])
        v.memset(ones_ap, 1.0)
        last = None
        for i in range(2):
            e_ones = bass.AP(eaug[i].ap().tensor, 256, [[4 * 257, 128], [257, 4]])
            last = v.memset(e_ones, 1.0)
        last.then_inc(s_vset.h, 1)
    S.on("vector", f_ones)
    s_vset.inc(1)

    for c in range(NCH):
        ln_chunk(c, with_xq=True)
        transpose_chunk(c, with_xq=True, paug_rdy=paug_rdy)
        ci = c % 4
        vb0, vb1 = (2, 3) if c % 2 == 0 else (6, 7)

        def f_vmm(t, c=c, ci=ci, vb0=vb0, vb1=vb1):
            mm = None
            for din in range(8):
                lhs = xnf[:, din * 512 + ci * 128: din * 512 + (ci + 1) * 128]
                mm = nc.tensor.matmul(pb[vb0][:, :], lhs, wvbf[:, din * 1024:din * 1024 + 512],
                                      start=(din == 0), stop=(din == 7))
                mm = nc.tensor.matmul(pb[vb1][:, :], lhs, wvbf[:, din * 1024 + 512:(din + 1) * 1024],
                                      start=(din == 0), stop=(din == 7))
            mm.then_inc(s_vmm.h, 1)
        S.on("tensor", f_vmm, waits=[(s_ev, s_ev.n), (s_ldwv, wv_rdy),
                                     (s_vev, max(0, s_vev.n - 1))])
        s_vmm.inc(1)

        def f_vev(v, c=c, vb0=vb0, vb1=vb1):
            dstA = bass.AP(vaug.ap().tensor, c * 1040, [[NCH * 1040, 128], [65, 8], [1, 64]])
            dstB = bass.AP(vaug.ap().tensor, c * 1040 + 8 * 65, [[NCH * 1040, 128], [65, 8], [1, 64]])
            v.tensor_copy(dstA, pb[vb0].ap().rearrange("p (h d) -> p h d", d=64))
            v.tensor_copy(dstB, pb[vb1].ap().rearrange("p (h d) -> p h d", d=64)).then_inc(s_vev.h, 1)
        S.on("vector", f_vev, waits=[(s_vmm, s_vmm.n), (s_vset, 1)])
        s_vev.inc(1)

        if ci == 3:
            spill_super(c // 4, with_xq=True)


    w1buf = place("w1buf", [128, HID], F32)
    wabs_scr = place("wabs_scr", [128, 1024], BF16)
    wmean2 = place("wmean2", [128, 80], F32)

    def wmean_pass2():
        w1s_flat = w1s_d.flatten().rearrange("(a b) -> a b", a=D)
        for wi, wd, nch_w, ncolw, base in ((0, w1s_flat, 8, HID, 0), (1, w2t_d, 32, D, 32)):
            for ch in range(nch_w):
                def f_ld(g, ch=ch, wd=wd, ncolw=ncolw):
                    g.dma_start(out=w1buf[:, 0:ncolw],
                                in_=wd[ch * 128:(ch + 1) * 128, 0:ncolw]).then_inc(s_w1m.h, 16)
                S.on("gpsimd", f_ld, waits=[(s_w1a, s_w1a.n)])
                s_w1m.inc(16)

                npc = ncolw // 1024
                def f_abs(sc, ch=ch, ncolw=ncolw, base=base, npc=npc, wi=wi):
                    for pz in range(npc):
                        col = base + (ch * npc + pz if wi == 0 else ch)
                        sc.drain()
                        sc.activation(out=wabs_scr[:, 0:1024],
                                      in_=w1buf[:, pz * 1024:(pz + 1) * 1024], func=AF.Abs,
                                      accum_out=wmean2[:, col:col + 1]).then_inc(s_w1a.h, 1)
                S.on("scalar", f_abs, waits=[(s_w1m, s_w1m.n)])
                s_w1a.inc(npc)
        def f_comb(v):
            v.reduce_sum(wmean2[:, 64:65], wmean2[:, 0:32], AX.X)
            v.reduce_sum(wmean2[:, 65:66], wmean2[:, 32:64], AX.X).then_inc(s_wmean.h, 1)
        S.on("vector", f_comb, waits=[(s_w1a, s_w1a.n)])
        s_wmean.inc(1)
        def f_tr(t):
            nc.tensor.transpose(pb[4][0:2, 0:128], wmean2[:, 64:66], ident32m[:, :]).then_inc(s_wq1.h, 1)
        S.on("tensor", f_tr, waits=[(s_wmean, s_wmean.n), (s_id32, id32m_rdy)])
        s_wq1.inc(1)

        def f_fin(v):
            v.reduce_sum(small[0:2, 42:43], pb[4][0:2, 0:128], AX.X)
            v.drain()
            v.tensor_scalar_mul(small[0:2, 43:44], small[0:2, 42:43], 1.0 / (D * HID))
            v.drain()
            v.tensor_scalar_max(small[0:2, 44:45], small[0:2, 43:44], 1e-5)
            v.drain()
            v.reciprocal(small[0:2, 45:46], small[0:2, 44:45]).then_inc(s_wmean.h, 1)
        S.on("vector", f_fin, waits=[(s_wq1, s_wq1.n)])
        s_wmean.inc(1)
        def f_bc(sy):
            sy.dma_start(out=rowd[15, 1:3], in_=small[0:2, 45:46]).then_inc(s_wmb.h, 16)
            sy.dma_start(out=rowd[15, 3:5], in_=small[0:2, 44:45]).then_inc(s_wmb.h, 16)
            sy.wait_ge(s_wmb.h, 32)
            sy.dma_start(out=wmean2[:, 66:67],
                         in_=bass.AP(rowd.tensor, 15 * T + 1, [[0, 128], [1, 1]])).then_inc(s_wmb.h, 16)
            sy.dma_start(out=wmean2[:, 67:68],
                         in_=bass.AP(rowd.tensor, 15 * T + 2, [[0, 128], [1, 1]])).then_inc(s_wmb.h, 16)
            sy.dma_start(out=wmean2[:, 68:69],
                         in_=bass.AP(rowd.tensor, 15 * T + 3, [[0, 128], [1, 1]])).then_inc(s_wmb.h, 16)
            sy.dma_start(out=wmean2[:, 69:70],
                         in_=bass.AP(rowd.tensor, 15 * T + 4, [[0, 128], [1, 1]])).then_inc(s_wmb.h, 16)
        S.on("sync", f_bc, waits=[(s_wmean, s_wmean.n)])
        s_wmb.inc(6 * 16)
        return s_wmb.n

    ident32m = place("ident32m", [128, 128], F32)
    def f_ld32m(g):
        g.dma_start(out=ident32m[:], in_=identbf_d[:, :]).then_inc(s_id32.h, 16)
    S.on("gpsimd", f_ld32m)
    id32m_rdy = s_id32.inc(16)

    wmb_rdy = wmean_pass2()

    w2str = overlay0("w2str", [128, D], F32, "w1buf", 0)
    w2strT = overlay0("w2strT", [128, D], F32, "w1buf", 4096)
    w2qbuf = [overlay0(f"w2qbuf{i}", [128, D], BF16, "w1buf", 8192 + i * 2048) for i in range(2)]
    def f_mg2(v):
        v.memset(small[:, 61:62], MAGIC).then_inc(s_wmean.h, 1)
    S.on("vector", f_mg2, waits=[(s_wmb, wmb_rdy)])
    mg2_rdy = s_wmean.inc(1)

    for hc in range(HID // 128):
        p2i = hc % 2

        def f_w2ld(g, hc=hc):
            g.dma_start(out=w2str[:, :],
                        in_=w2t_d[hc * 128:(hc + 1) * 128, :]).then_inc(s_wst2.h, 16)
        S.on("gpsimd", f_w2ld, waits=[(s_wq1, s_wq1.n), (s_w1a, s_w1a.n), (s_w1m, s_w1m.n)])
        s_wst2.inc(16)

        def f_w2act(sc, hc=hc):
            sc.activation(out=w2strT[:], in_=w2str[:], func=AF.Identity,
                          bias=small[:, 61:62], scale=wmean2[:, 67:68]).then_inc(s_wq1.h, 1)
        S.on("scalar", f_w2act, waits=[(s_wst2, s_wst2.n), (s_wmean, mg2_rdy), (s_wq2g, s_wq2g.n)])
        s_wq1.inc(1)

        def f_w2vec(v, hc=hc, p2i=p2i):
            v.tensor_scalar(w2strT[:], w2strT[:], MAGIC, -1.0, ALU.subtract, ALU.max)
            v.drain()
            v.tensor_scalar_min(w2qbuf[p2i][:], w2strT[:], 1.0).then_inc(s_wq2g.h, 1)
        wv2 = [(s_wq1, s_wq1.n)]
        if hc >= 2:
            wv2.append((s_w2dp[p2i], s_w2dp[p2i].n))
        S.on("vector", f_w2vec, waits=wv2)
        s_wq2g.inc(1)

        def f_w2d(g, hc=hc, p2i=p2i):
            g.dma_start(out=w2qd[hc, :, :], in_=w2qbuf[p2i][:, :]).then_inc(s_w2dp[p2i].h, 16)
        S.on("gpsimd", f_w2d, waits=[(s_wq2g, s_wq2g.n)])
        s_w2dp[p2i].inc(16)

    def overlay(name, shape, dtype, base_tensor, delta=0):
        base_off = OFFSETS[base_tensor]
        return sb_at(name, shape, dtype, offset=base_off + delta)

    ksqA = place("ksqA", [128, 512], BF16)
    ksqB = place("ksqB", [128, 512], BF16)

    vaug_done = s_vev.n
    spill_done = s_spill.n

    for sw in range(4):
        d0 = 2 * sw
        kev_sweep_base = s_kev.n
        kproj_pair(d0, wk_rdy)

        for sg in range(NSUP):
            def f_ksq(sc, sg=sg, kb=kev_sweep_base):
                sc.activation(out=ksqA[:], in_=kpair[:, sg * 512:(sg + 1) * 512],
                              func=AF.Square)
                sc.activation(out=ksqB[:], in_=kpair[:, T + sg * 512: T + (sg + 1) * 512],
                              func=AF.Square).then_inc(s_ksq.h, 1)
            S.on("scalar", f_ksq, waits=[(s_kev, kev_sweep_base + sg + 1), (s_dg, s_dg.n)])
            s_ksq.inc(1)

            def f_dmm(t, sg=sg):
                nc.tensor.matmul(pb[4][0:4, 0:512], oblk[:, 0:4], ksqA[:], start=True, stop=False)
                nc.tensor.matmul(pb[4][0:4, 0:512], oblk[:, 4:8], ksqB[:], start=False, stop=True).then_inc(s_dg.h, 1)
            S.on("tensor", f_dmm, waits=[(s_ksq, s_ksq.n)])
            s_dg.inc(1)

            def f_dev(v, sg=sg):
                v.tensor_copy(negd_sb[0:4, 0:512], pb[4][0:4, 0:512]).then_inc(s_dge.h, 1)
            S.on("vector", f_dev, waits=[(s_dg, s_dg.n)])
            s_dge.inc(1)

            def f_dtr(t, sg=sg):
                last = None
                for ci in range(4):
                    last = nc.tensor.transpose(pb[5].ap().bitcast(F32)[:, ci * 4:(ci + 1) * 4],
                                               negd_sb[0:4, ci * 128:(ci + 1) * 128], ident32m[0:4, 0:4])
                last.then_inc(s_dg.h, 1)
            S.on("tensor", f_dtr, waits=[(s_dge, s_dge.n), (s_id32, id32m_rdy)])
            s_dg.inc(1)

            def f_dtev(v, sg=sg):
                v.tensor_copy(diagks[:, sg * 16:(sg + 1) * 16], pb[5][:, 0:16]).then_inc(s_dge.h, 1)
            S.on("vector", f_dtev, waits=[(s_dg, s_dg.n)])
            s_dge.inc(1)

        def f_comb(v):
            st_b = stabcol[:, 0:1].broadcast_to((128, 1, 4 * NCH))
            v.tensor_tensor(diagks.ap().rearrange("p (a b) -> p a b", a=1),
                            diagks.ap().rearrange("p (a b) -> p a b", a=1), st_b,
                            ALU.subtract).then_inc(s_dks.h, 1)
        S.on("vector", f_comb, waits=[(s_dge, s_dge.n), (s_stabb, stab_rdy)])
        s_dks.inc(1)

        ctx_mm_base = s_ctx.n
        for c in range(NCH):
            bk = (pb[4], pb[5]) if c % 2 == 0 else (pb[6], pb[7])

            def f_dk(t, c=c, bk=bk):
                for dd in range(2):
                    nc.tensor.matmul(bk[dd][:, :],
                                     kpair[:, dd * T + c * 128:dd * T + (c + 1) * 128],
                                     paug[:, :], start=True, stop=True).then_inc(s_dkmm.h, 1)
            S.on("tensor", f_dk, waits=[(s_dks, s_dks.n), (s_exp, s_exp.n - 2 if c >= 2 else 0),
                                        (s_kev, s_kev.n)])
            s_dkmm.inc(2)

            def f_exp(sc, c=c, bk=bk):
                eb = eaug[c % 2]
                for dd in range(2):
                    last = None
                    for hh in range(2):
                        last = sc.activation(out=eb[:, (dd * 2 + hh) * 257:(dd * 2 + hh) * 257 + 256],
                                             in_=bk[dd][:, hh * 256:(hh + 1) * 256], func=AF.Exp,
                                             bias=diagks[:, c * 4 + dd * 2 + hh:c * 4 + dd * 2 + hh + 1])
                    last.then_inc(s_exp.h, 1)
            ew = [(s_dkmm, s_dkmm.n)]
            if c >= 2:
                ew.append((s_ctx, s_ctx.n - 1))
            ew.append((s_vset, 1))
            S.on("scalar", f_exp, waits=ew)
            s_exp.inc(2)

            def f_ctx(t, c=c, sw=sw):
                eb = eaug[c % 2]
                mm = None
                for j in range(4):
                    h = 4 * sw + j
                    mm = nc.tensor.matmul(pb[j][0:65, 0:257],
                                          vaug[:, c * 1040 + h * 65: c * 1040 + (h + 1) * 65],
                                          eb[:, j * 257:(j + 1) * 257],
                                          start=(c == 0), stop=(c == NCH - 1))
                mm.then_inc(s_ctx.h, 1)
            S.on("tensor", f_ctx, waits=[(s_exp, s_exp.n), (s_vev, vaug_done),
                                         (s_ctxe, s_ctxe.n)])
            s_ctx.inc(1)

        def f_ctxe(v, sw=sw):
            last = None
            for j in range(4):
                h = 4 * sw + j
                last = v.tensor_copy(ctx_sb[0:65, h * 257:(h + 1) * 257], pb[j][0:65, 0:257])
            last.then_inc(s_ctxe.h, 1)
        S.on("vector", f_ctxe, waits=[(s_ctx, s_ctx.n)])
        s_ctxe.inc(1)

    def f_rhs_ones(v):
        ones_ap = bass.AP(rhs_sb.ap().tensor, 65, [[H * 2 * 66, 128], [66, H * 2]])
        v.memset(ones_ap, 1.0).then_inc(s_ctxe.h, 1)
    S.on("vector", f_rhs_ones)
    s_ctxe.inc(1)

    ctxe_all = s_ctxe.n

    def f_vsfold(v):
        ctx3 = ctx_sb.ap().rearrange("p (h m) -> p h m", h=H)
        small3 = small.ap().rearrange("p (a b) -> p a b", b=1)
        v.tensor_scalar_mul(small3[0:64, 16:32, 0:1], ctx3[0:64, :, 256:257], EPS_K)
        v.drain()
        v.tensor_tensor(ctx3[0:64, :, 0:256], ctx3[0:64, :, 0:256],
                        small3[0:64, 16:32, 0:1].broadcast_to((64, H, 256)),
                        ALU.add).then_inc(s_ctxe.h, 1)
    S.on("vector", f_vsfold, waits=[(s_ctxe, ctxe_all)])
    vf_rdy = s_ctxe.inc(1)

    for h in range(H):
        for mc in range(2):
            def f_rt(t, h=h, mc=mc):
                nc.tensor.transpose(pb[4].ap().bitcast(F32)[:, 0:65],
                                    ctx_sb[0:65, h * 257 + mc * 128: h * 257 + (mc + 1) * 128],
                                    ident32m[0:65, 0:65]).then_inc(s_rhst.h, 1)
            S.on("tensor", f_rt, waits=[(s_ctxe, s_ctxe.n)])
            s_rhst.inc(1)

            def f_re(v, h=h, mc=mc):
                v.tensor_copy(rhs_sb[:, (h * 2 + mc) * 66:(h * 2 + mc) * 66 + 65],
                              pb[4].ap().bitcast(F32)[:, 0:65]).then_inc(s_ctxe.h, 1)
            S.on("vector", f_re, waits=[(s_rhst, s_rhst.n)])
            s_ctxe.inc(1)
    rhs_rdy = s_ctxe.n

    wqbf = overlay("wqbf", [128, 8 * 1024], BF16, "wkbf")
    wobf = overlay("wobf", [128, 8 * 1024], BF16, "wvbf")
    qf = overlay("qf", [128, 8 * 512], BF16, "vaug", 0)
    e_tm = overlay("e_tm", [128, H * M], BF16, "vaug", 8192)
    eqt = overlay("eqt", [128, H * M], BF16, "vaug", 16384)
    diagq = overlay("diagq", [128, 64], F32, "vaug", 24576)
    attn_tm = overlay("attn_tm", [128, D], BF16, "vaug", 24832)
    attnf = overlay("attnf", [128, D], BF16, "vaug", 26880)

    def f_ldwq(g):
        for dinc in range(8):
            g.dma_start(out=wqbf[:, dinc * 1024:(dinc + 1) * 1024],
                        in_=wq_d[dinc * 128:(dinc + 1) * 128, :]).then_inc(s_ldwq.h, 16)
        for dinc in range(8):
            g.dma_start(out=wobf[:, dinc * 1024:(dinc + 1) * 1024],
                        in_=wo_d[dinc * 128:(dinc + 1) * 128, :]).then_inc(s_ldwo.h, 16)
    S.on("gpsimd", f_ldwq, waits=[(s_kmm, s_kmm.n), (s_vmm, s_vmm.n)])
    s_ldwq.inc(8 * 16)
    s_ldwo.inc(8 * 16)
    wq_rdy = s_ldwq.n
    wo_rdy = s_ldwo.n

    for sg in range(NSUP):
        xnsup = xnsupb[sg % 2]
        sxn = s_xnsp[sg % 2]

        def f_ld(sy, sg=sg, xnsup=xnsup, sxn=sxn):
            for j in range(8):
                sy.dma_start(out=xnsup[:, j * 512:(j + 1) * 512],
                             in_=xnd[j, :, sg * 512:(sg + 1) * 512]).then_inc(sxn.h, 16)
        S.on("sync", f_ld, waits=[(s_qmm, max(0, s_qmm.n - 8)), (s_spill, spill_done),
                                  (s_kmm, s_kmm.n), (s_wq2g, s_wq2g.n)])
        sxn.inc(8 * 16)

        def f_qmm(t, sg=sg, qevb=s_qev.n, xnsup=xnsup):
            for dout in range(8):
                if dout >= 2:
                    t.wait_ge(s_qev.h, qevb + dout - 1)
                ps = pb[2 + dout % 2]
                mm = None
                for din in range(8):
                    mm = nc.tensor.matmul(
                        ps[:, :],
                        wqbf[:, din * 1024 + dout * 128: din * 1024 + (dout + 1) * 128],
                        xnsup[:, din * 512:(din + 1) * 512],
                        start=(din == 0), stop=(din == 7))
                mm.then_inc(s_qmm.h, 1)
        S.on("tensor", f_qmm, waits=[(sxn, sxn.n), (s_ldwq, wq_rdy), (s_qev, s_qev.n)])
        qmm_base = s_qmm.n
        s_qmm.inc(8)

        def f_qev(v, qmm_base=qmm_base):
            for dout in range(8):
                v.wait_ge(s_qmm.h, qmm_base + dout + 1)
                v.tensor_copy(qf[:, dout * 512:(dout + 1) * 512],
                              pb[2 + dout % 2][:, :]).then_inc(s_qev.h, 1)
        S.on("vector", f_qev, waits=([(s_ctx, s_ctx.n)] if sg == 0 else []))
        s_qev.inc(8)

        qdmm_base = s_dg.n
        for dout in range(8):
            def f_qsq(sc, sg=sg, dout=dout, qdmm_base=qdmm_base):
                sc.activation(out=ksqA[:] if dout % 2 == 0 else ksqB[:],
                              in_=qf[:, dout * 512:(dout + 1) * 512],
                              func=AF.Square).then_inc(s_ksq.h, 1)
            sqw = [(s_qev, s_qev.n - (7 - dout)), (s_ctxe, rhs_rdy)]
            if dout >= 2:
                sqw.append((s_dg, qdmm_base + dout - 1))
            else:
                sqw.append((s_dg, s_dg.n))
            S.on("scalar", f_qsq, waits=sqw)
            s_ksq.inc(1)

            def f_qdmm(t, dout=dout):
                nc.tensor.matmul(pb[4][0:16, 0:512], oblk[:, 8 + dout * 16: 8 + (dout + 1) * 16],
                                 ksqA[:] if dout % 2 == 0 else ksqB[:],
                                 start=(dout == 0), stop=(dout == 7)).then_inc(s_dg.h, 1)
            S.on("tensor", f_qdmm, waits=[(s_ksq, s_ksq.n), (s_dge, s_dge.n)])
            s_dg.inc(1)

        def f_qdev(v):
            v.tensor_copy(negd_sb[0:16, 0:512], pb[4][0:16, 0:512]).then_inc(s_dge.h, 1)
        S.on("vector", f_qdev, waits=[(s_dg, s_dg.n)])
        s_dge.inc(1)

        def f_qdtr(t):
            last = None
            for ci in range(4):
                last = nc.tensor.transpose(pb[5].ap().bitcast(F32)[:, ci * 16:(ci + 1) * 16],
                                           negd_sb[0:16, ci * 128:(ci + 1) * 128], ident32m[0:16, 0:16])
            last.then_inc(s_dg.h, 1)
        S.on("tensor", f_qdtr, waits=[(s_dge, s_dge.n)])
        s_dg.inc(1)

        def f_qdtev(v):
            v.tensor_copy(diagq[:, 0:64], pb[5].ap().bitcast(F32)[:, 0:64]).then_inc(s_dge.h, 1)
        S.on("vector", f_qdtev, waits=[(s_dg, s_dg.n)])
        s_dge.inc(1)

        for ci in range(4):
            c = sg * 4 + ci

            for g in range(4):
                bk = (pb[2], pb[3]) if g % 2 == 0 else (pb[6], pb[7])

                def f_dq(t, g=g, ci=ci, bk=bk):
                    for dd in range(2):
                        pair = g * 2 + dd
                        nc.tensor.matmul(bk[dd][:, :],
                                         qf[:, pair * 512 + ci * 128: pair * 512 + (ci + 1) * 128],
                                         paug[:, :], start=True, stop=True).then_inc(s_dqmm.h, 1)
                dqw = [(s_qev, s_qev.n), (s_dge, s_dge.n)]
                if g >= 2:
                    dqw.append((s_eq, s_eq.n - 2))
                else:
                    dqw.append((s_eq, s_eq.n))
                    dqw.append((s_asm, s_asm.n))
                if g % 2 == 1:
                    dqw.append((s_xae, s_xae.n))
                S.on("tensor", f_dq, waits=dqw)
                s_dqmm.inc(2)

                cslot = 52 + (g % 2) * 4

                nslot = 36 + (g % 2) * 12

                def f_dqsub(v, g=g, ci=ci, bk=bk, cslot=cslot, nslot=nslot):
                    for dd in range(2):
                        v.reduce_max(small[:, nslot + dd * 2:nslot + dd * 2 + 2],
                                     bk[dd].ap().rearrange("p (h m) -> p h m", h=2),
                                     AX.X, negate=True)
                    v.drain()
                    v.tensor_add(small[:, cslot:cslot + 4], small[:, nslot:nslot + 4],
                                 diagq[:, ci * 16 + g * 4: ci * 16 + g * 4 + 4]).then_inc(s_dqsub.h, 1)
                S.on("vector", f_dqsub, waits=[(s_dqmm, s_dqmm.n)])
                s_dqsub.inc(1)

                def f_eq(sc, g=g, bk=bk, cslot=cslot):
                    for dd in range(2):
                        last = None
                        for hh in range(2):
                            last = sc.activation(out=e_tm[:, ((g * 2 + dd) * 2 + hh) * 256:((g * 2 + dd) * 2 + hh + 1) * 256],
                                                 in_=bk[dd][:, hh * 256:(hh + 1) * 256], func=AF.Exp,
                                                 bias=small[:, cslot + dd * 2 + hh:cslot + dd * 2 + hh + 1])
                        last.then_inc(s_eq.h, 1)
                S.on("scalar", f_eq, waits=[(s_dqsub, s_dqsub.n)])
                s_eq.inc(2)

            def f_eqt(t, c=c):
                last = None
                for j in range(32):
                    dstT = pb[4 + j // 8]
                    jj = j % 8
                    last = nc.tensor.transpose(
                        dstT.ap().bitcast(BF16)[:, jj * 128:(jj + 1) * 128],
                        e_tm[:, j * 128:(j + 1) * 128], identbf[:])
                last.then_inc(s_eqt.h, 1)
            S.on("tensor", f_eqt, waits=[(s_eq, s_eq.n), (s_eqe, s_eqe.n), (s_dg, s_dg.n)])
            s_eqt.inc(1)

            def f_eqe(v):
                last = None
                for q in range(4):
                    last = v.tensor_scalar_add(eqt[:, q * 1024:(q + 1) * 1024],
                                               pb[4 + q].ap().bitcast(BF16)[:, :], EPS_K)
                last.then_inc(s_eqe.h, 1)
            S.on("vector", f_eqe, waits=[(s_eqt, s_eqt.n)])
            s_eqe.inc(1)

            def f_amm(t, c=c):
                mm = None
                for h in range(H):
                    dstT = pb[h // 4]
                    col = (h % 4) * 128
                    for mc in range(2):
                        mm = nc.tensor.matmul(
                            dstT[:, col:col + 66],
                            eqt[:, h * 256 + mc * 128: h * 256 + (mc + 1) * 128],
                            rhs_sb[:, (h * 2 + mc) * 66:(h * 2 + mc + 1) * 66],
                            start=(mc == 0), stop=(mc == 1))
                mm.then_inc(s_amm.h, 1)
            S.on("tensor", f_amm, waits=[(s_eqe, s_eqe.n), (s_ctxe, rhs_rdy),
                                         (s_asm, s_asm.n), (s_eq, s_eq.n)])
            s_amm.inc(1)

            def f_asm(v, c=c):
                for q in range(4):
                    ps = pb[q]
                    dr = bass.AP(ps.ap().tensor, 64, [[512, 128], [128, 4], [1, 2]])
                    v.tensor_copy(small.ap().rearrange("p (a b) -> p a b", b=2)[:, q * 4:(q + 1) * 4, :], dr)
                v.drain()
                denv = bass.AP(small.ap().tensor, 0, [[64, 128], [2, 16], [1, 1]])
                rrv = bass.AP(small.ap().tensor, 1, [[64, 128], [2, 16], [1, 1]])
                v.scalar_tensor_tensor(
                    small.ap().rearrange("p (a b) -> p a b", b=1)[:, 32:48, 0:1],
                    rrv, EPS_K * T, denv, ALU.mult, ALU.add)
                v.drain()
                v.reciprocal(small[:, 16:32], small[:, 32:48])
                v.drain()
                last = None
                for q in range(4):
                    ps = pb[q]
                    num = bass.AP(ps.ap().tensor, 0, [[512, 128], [128, 4], [1, 64]])
                    last = v.tensor_tensor(
                        attn_tm.ap().rearrange("p (h d) -> p h d", d=64)[:, q * 4:(q + 1) * 4, :],
                        num,
                        small.ap().rearrange("p (a b) -> p a b", b=1)[:, 16 + q * 4:16 + (q + 1) * 4, 0:1].broadcast_to((128, 4, 64)),
                        ALU.mult)
                last.then_inc(s_asm.h, 1)
            S.on("vector", f_asm, waits=[(s_amm, s_amm.n), (s_wmb, wmb_rdy),
                                         (s_stabb, stab_rdy)])
            s_asm.inc(1)

            def f_att(t, c=c):
                last = None
                for j in range(8):
                    last = nc.tensor.transpose(pb[4].ap().bitcast(BF16)[:, j * 128:(j + 1) * 128],
                                               attn_tm[:, j * 128:(j + 1) * 128], identbf[:])
                last.then_inc(s_att.h, 1)
            S.on("tensor", f_att, waits=[(s_asm, s_asm.n), (s_ate, s_ate.n), (s_eqe, s_eqe.n)])
            s_att.inc(1)

            def f_ate(v):
                v.tensor_copy(attnf[:], pb[4].ap().bitcast(BF16)[:, 0:1024]).then_inc(s_ate.h, 1)
            S.on("vector", f_ate, waits=[(s_att, s_att.n)])
            s_ate.inc(1)

            def f_wo(t, c=c):
                mm = None
                for half in range(2):
                    ps = pb[6 + half]
                    for din in range(8):
                        mm = nc.tensor.matmul(
                            ps[:, :], attnf[:, din * 128:(din + 1) * 128],
                            wobf[:, din * 1024 + half * 512: din * 1024 + half * 512 + 512],
                            start=(din == 0), stop=False)
                    mm = nc.tensor.matmul(ps[:, :], onesrow_sb[0:1, :],
                                          bo_sb[0:1, half * 512:(half + 1) * 512],
                                          start=False, stop=True)
                mm.then_inc(s_womm.h, 1)
            S.on("tensor", f_wo, waits=[(s_ate, s_ate.n), (s_ldwo, wo_rdy),
                                        (s_ldm, ldm_rdy), (s_xae, s_xae.n)])
            s_womm.inc(1)

            sxr = s_xa if c % 2 == 0 else s_xb

            def f_xre(sy, c=c, sxr=sxr):
                sy.dma_start(out=xbuf[c % 2][:], in_=x_d[c * 128:(c + 1) * 128, :]).then_inc(sxr.h, 16)
            xw = [(s_xae, s_xae.n - 1)] if c >= 2 else [(s_u, NCH), (s_gp, NCH)]
            S.on("sync", f_xre, waits=xw)
            sxr.inc(16)

            def f_xae(v, c=c, sxr=sxr, xr=sxr.n):
                v.wait_ge(sxr.h, xr)
                for half in range(2):
                    last = v.tensor_add(out_sb[:, half * 512:(half + 1) * 512], pb[6 + half][:, :],
                                        xbuf[c % 2][:, half * 512:(half + 1) * 512])
                last.then_inc(s_xae.h, 1)
            S.on("vector", f_xae, waits=[(s_womm, s_womm.n), (s_xad, s_xad.n)])
            s_xae.inc(1)

            def f_xad(sy, c=c):
                sy.dma_start(out=xattnd[c * 128:(c + 1) * 128, :], in_=out_sb[:]).then_inc(s_xad.h, 16)
            S.on("sync", f_xad, waits=[(s_xae, s_xae.n)])
            s_xad.inc(16)

    xqfull = [overlay(f"xqfull{j}", [128, T], BF16, "vaug", j * 8224) for j in range(8)]
    tmpc = overlay("tmpc", [128, T], F32, "wkbf")
    mxf = overlay("mxf", [128, T], BF16, "wvbf")
    mnf = overlay("mnf", [128, T], BF16, "wvbf", 8192)
    h1sb = [overlay("h1s", [128, T + 2], F32, "kpair"),
            overlay("h1s2", [128, T + 2], F32, "ctx_sb")]
    h2bf = overlay("h2bf", [128, T], BF16, "w1buf")
    wstr = [overlay(f"wstr{i}", [128, 1024], F32, "xnf", i * 4096) for i in range(2)]
    wstrT = overlay("wstrT", [128, 1024], F32, "xnf", 8192)
    w1qb = [overlay(f"w1qb{i}", [128, 1024], BF16, "xqf", 4096 + i * 2048) for i in range(2)]
    cwsb = place("cwsb", [128, 3 * 32], F32)
    cbsb = place("cbsb", [128, 32], F32)

    NHC = HID // 128

    p2_done = [(s_womm, s_womm.n), (s_dqmm, s_dqmm.n), (s_amm, s_amm.n),
               (s_eqt, s_eqt.n), (s_att, s_att.n), (s_xae, s_xae.n),
               (s_asm, s_asm.n), (s_ate, s_ate.n), (s_ctx, s_ctx.n)]

    def f_ldcw(g):
        g.dma_start(out=cwsb[:], in_=cw_d[:, :]).then_inc(s_ldm.h, 16)
        g.dma_start(out=cbsb[:], in_=cb_d[:, :]).then_inc(s_ldm.h, 16)
    S.on("gpsimd", f_ldcw)
    cw_rdy = s_ldm.inc(32)

    def f_cwsc(v):
        v.tensor_scalar_mul(cwsb[:], cwsb[:], wmean2[:, 68:69])
        v.drain()
        v.memset(small[:, 60:61], MAGIC)
        v.memset(mxf[:], -1e30)
        v.memset(mnf[:], 1e30).then_inc(s_mxr.h, 1)
    S.on("vector", f_cwsc, waits=[(s_ldm, cw_rdy), (s_wmb, wmb_rdy),
                                  (s_womm, s_womm.n), (s_qmm, s_qmm.n)])
    cw_sc_rdy = s_mxr.inc(1)

    def f_xqr(sy):
        for j in range(8):
            sy.dma_start(out=xqfull[j][:, :], in_=xqd[j, :, :]).then_inc(s_xqr.h, 16)
    S.on("sync", f_xqr, waits=p2_done + [(s_spill, s_spill.n)])
    s_xqr.inc(8 * 16)
    xq_rdy = s_xqr.n

    p1_kpair_done = [(s_dkmm, s_dkmm.n), (s_ksq, s_ksq.n)]

    def f_h1z(v):
        last = None
        for i in range(2):
            v.memset(h1sb[i][:, 0:1], 0.0)
            last = v.memset(h1sb[i][:, T + 1:T + 2], 0.0)
        last.then_inc(s_cv1.h, 1)
    S.on("vector", f_h1z, waits=[(s_ctxe, s_ctxe.n), (s_dkmm, s_dkmm.n), (s_ksq, s_ksq.n)])
    h1z_rdy = s_cv1.inc(1)

    for hc in range(NHC):
        p = hc % 2
        def f_wld(sy, hc=hc, p=p):
            for din in range(8):
                sy.dma_start(out=wstr[p][:, din * 128:(din + 1) * 128],
                             in_=w1s_d[hc, din * 128:(din + 1) * 128, :]).then_inc(s_wst.h, 16)
        wldw = ([(s_wq2, s_wq2.n - 1)] if hc >= 2
                else [(s_vmm, s_vmm.n), (s_spill, s_spill.n), (s_w1a, s_w1a.n),
                      (s_w1m, s_w1m.n)])
        S.on("sync", f_wld, waits=wldw)
        s_wst.inc(8 * 16)

        def f_wact(sc, p=p):
            sc.activation(out=wstrT[:], in_=wstr[p][:], func=AF.Identity,
                          bias=small[:, 60:61], scale=wmean2[:, 66:67]).then_inc(s_wq1.h, 1)
        S.on("scalar", f_wact, waits=[(s_wst, s_wst.n), (s_mxr, cw_sc_rdy),
                                      (s_wq2, s_wq2.n)])
        s_wq1.inc(1)

        def f_wvec(v, p=p):
            v.tensor_scalar(wstrT[:], wstrT[:], MAGIC, -1.0, ALU.subtract, ALU.max)
            v.drain()
            v.tensor_scalar_min(w1qb[p][:], wstrT[:], 1.0).then_inc(s_wq2.h, 1)
        S.on("vector", f_wvec, waits=[(s_wq1, s_wq1.n)])
        s_wq2.inc(1)

        def f_h1(t, hc=hc, p=p, evb=s_h1e.n):
            mm = None
            for j in range(9):
                lo = j * 512 - 1
                nn = min(512, T - max(lo, 0)) if j > 0 else 511
                src0 = max(lo, 0)
                ps = pb[j % 4]
                if j >= 4:
                    t.wait_ge(s_h1e.h, evb + j - 3)
                for din in range(8):
                    mm = nc.tensor.matmul(
                        ps[:, 0:nn], w1qb[p][:, din * 128:(din + 1) * 128],
                        xqfull[din][:, src0:src0 + nn],
                        start=(din == 0), stop=(din == 7))
                mm.then_inc(s_h1.h, 1)
        S.on("tensor", f_h1, waits=[(s_wq2, s_wq2.n), (s_xqr, xq_rdy)])
        h1_base = s_h1.n
        s_h1.inc(9)

        h1s = h1sb[hc % 2]

        def f_h1e(sc, hc=hc, h1_base=h1_base, h1s=h1s):
            for j in range(9):
                lo = j * 512 - 1
                nn = min(512, T - max(lo, 0)) if j > 0 else 511
                dst0 = max(lo, 0) + 1
                sc.wait_ge(s_h1.h, h1_base + j + 1)
                sc.activation(out=h1s[:, dst0:dst0 + nn], in_=pb[j % 4][:, 0:nn],
                              func=AF.Identity).then_inc(s_h1e.h, 1)
        wh1e = ([(s_cv3, s_cv3.n - 1), (s_cv2, s_cv2.n - 1)] if hc >= 2
                else [(s_cv1, h1z_rdy), (s_ctxe, s_ctxe.n)] + p1_kpair_done)
        S.on("scalar", f_h1e, waits=wh1e)
        s_h1e.inc(9)

        def f_cv1(v, hc=hc, h1s=h1s):
            v.tensor_scalar(tmpc[:, :], h1s[:, 0:T], cwsb[:, hc * 3:hc * 3 + 1],
                            None, ALU.mult).then_inc(s_cv1.h, 1)
            v.drain()
        S.on("vector", f_cv1, waits=[(s_h1e, s_h1e.n), (s_sil, s_sil.n)])
        s_cv1.inc(1)

        def f_cv2(v, hc=hc, h1s=h1s):
            v.scalar_tensor_tensor(tmpc[:, :], h1s[:, 1:T + 1],
                                   cwsb[:, hc * 3 + 1:hc * 3 + 2],
                                   tmpc[:, :], ALU.mult, ALU.add).then_inc(s_cv2.h, 1)
            v.drain()
        S.on("vector", f_cv2, waits=[(s_cv1, s_cv1.n)])
        s_cv2.inc(1)

        def f_cv3(v, hc=hc, h1s=h1s):
            v.scalar_tensor_tensor(tmpc[:, :], h1s[:, 2:T + 2],
                                   cwsb[:, hc * 3 + 2:hc * 3 + 3],
                                   tmpc[:, :], ALU.mult, ALU.add).then_inc(s_cv3.h, 1)
            v.drain()
        S.on("vector", f_cv3, waits=[(s_cv2, s_cv2.n)])
        s_cv3.inc(1)

        def f_sil(sc, hc=hc):
            sc.activation(out=h2bf[:], in_=tmpc[:], func=AF.Silu,
                          bias=cbsb[:, hc:hc + 1]).then_inc(s_sil.h, 1)
        S.on("scalar", f_sil, waits=[(s_cv3, s_cv3.n), (s_h2d, s_h2d.n), (s_w1a, s_w1a.n)])
        s_sil.inc(1)

        def f_mx(v, hc=hc):
            v.tensor_max(mxf[:], mxf[:], h2bf[:])
            v.tensor_tensor(mnf[:], mnf[:], h2bf[:], ALU.min).then_inc(s_mx.h, 1)
        S.on("vector", f_mx, waits=[(s_sil, s_sil.n), (s_mxr, cw_sc_rdy)])
        s_mx.inc(1)

        def f_h2d(sy, hc=hc):
            sy.dma_start(out=h2d[hc, :, :], in_=h2bf[:, :]).then_inc(s_h2d.h, 16)
        S.on("sync", f_h2d, waits=[(s_mx, s_mx.n)])
        s_h2d.inc(16)


    def f_mxc(v, mxall=s_mx.n):
        v.wait_ge(s_mx.h, mxall)
        v.tensor_scalar_mul(mnf[:], mnf[:], -1.0)
        v.drain()
        v.tensor_max(mxf[:], mxf[:], mnf[:]).then_inc(s_mx.h, 1)
    S.on("vector", f_mxc)
    mx_comb = s_mx.inc(1)

    def f_mxt(t):
        for grp in range(8):
            last = None
            for q in range(4):
                j = grp * 4 + q
                last = nc.tensor.transpose(pb[2].ap().bitcast(BF16)[:, q * 128:(q + 1) * 128],
                                           mxf[:, j * 128:(j + 1) * 128], identbf[:, :])
            last.then_inc(s_mxt.h, 1)
            t.wait_ge(s_mxr.h, cw_sc_rdy + 1 + grp)
    S.on("tensor", f_mxt, waits=[(s_h1, s_h1.n), (s_asm, s_asm.n), (s_mx, mx_comb)])
    mxt_base = s_mxt.n
    s_mxt.inc(8)

    def f_mxr(v, mxt_base=mxt_base):
        for grp in range(8):
            v.wait_ge(s_mxt.h, mxt_base + grp + 1)
            v.reduce_max(statc[:, 32 + grp * 4:32 + (grp + 1) * 4],
                         pb[2].ap().bitcast(BF16)[:, 0:512].rearrange("p (a b) -> p a b", b=128),
                         AX.X).then_inc(s_mxr.h, 1)
        v.drain()
        v.tensor_scalar_max(statc[:, 64:96], statc[:, 32:64], 1e-5)
        v.drain()
        v.reciprocal(statc[:, 96:128], statc[:, 64:96])
        v.drain()
        v.tensor_scalar_mul(statc[:, 96:128], statc[:, 96:128], 127.0)
        v.drain()
        v.tensor_scalar(statc[:, 64:96], statc[:, 64:96], 1.0 / 127.0,
                        wmean2[:, 69:70], ALU.mult, ALU.mult).then_inc(s_mxr.h, 1)
    S.on("vector", f_mxr, waits=[(s_wmb, wmb_rdy)])
    s_mxr.inc(9)
    s2_rdy = s_mxr.n

    s2rep = overlay("s2rep", [128, T], F32, "kpair")

    def f_s2t(t):
        nc.tensor.transpose(pb[3][0:32, 0:128], statc[:, 96:128], ident32m[:, :]).then_inc(s_mxt.h, 1)
    S.on("tensor", f_s2t, waits=[(s_mxr, s2_rdy)])
    s_mxt.inc(1)

    def f_s2e(v):
        v.tensor_copy(negd_sb[0:32, 0:128], pb[3][0:32, 0:128]).then_inc(s_mxr.h, 1)
    S.on("vector", f_s2e, waits=[(s_mxt, s_mxt.n)])
    s_mxr.inc(1)

    def f_s2d(sy):
        sy.dma_start(out=rowd[1:2, 0:T], in_=negd_sb[0:32, 0:128]).then_inc(s_s2d.h, 16)
        sy.wait_ge(s_s2d.h, 16)
        src = bass.AP(rowd.tensor, 1 * T, [[0, 128], [1, T]])
        sy.dma_start(out=s2rep[:, :], in_=src).then_inc(s_s2d.h, 16)
    S.on("sync", f_s2d, waits=[(s_mxr, s_mxr.n), (s_cv3, s_cv3.n), (s_cv2, s_cv2.n)])
    s_s2d.inc(32)
    s2rep_rdy = s_s2d.n

    w2qsb = [overlay(f"w2qsb{i}", [128, 1024], BF16, "vaug", i * 2048) for i in range(NHC)]
    h2q = ([overlay(f"h2q{i}", [128, 512], BF16, "wkbf", i * 1024) for i in range(16)] +
           [overlay(f"h2q{i+16}", [128, 512], BF16, "wvbf", i * 1024) for i in range(16)])
    h2raw = [overlay(f"h2raw{i}", [128, 512], BF16, "eaug0", i * 1024) for i in range(2)]
    h2rawf = [overlay(f"h2rawf{i}", [128, 512], F32, "eaug1", i * 2048) for i in range(2)]

    def f_w2r(sy):
        for hc in range(NHC):
            sy.dma_start(out=w2qsb[hc][:, :], in_=w2qd[hc, :, :]).then_inc(s_w2r.h, 16)
    S.on("sync", f_w2r, waits=[(s_h1, s_h1.n), (s_w2dp[0], s_w2dp[0].n), (s_w2dp[1], s_w2dp[1].n), (s_xqr, s_xqr.n)])
    s_w2r.inc(NHC * 16)
    w2r_rdy = s_w2r.n

    for grp in range(8):
        for hc in range(NHC):
            pq = hc % 2

            def f_h2r(sy, grp=grp, hc=hc, pq=pq, qa=s_qa.n):
                if grp > 0 or hc >= 2:
                    sy.wait_ge(s_qa.h, qa - 1)
                sy.dma_start(out=h2raw[pq][:, :],
                             in_=h2d[hc, :, grp * 512:(grp + 1) * 512]).then_inc(s_h2rp[pq].h, 16)
            w_first = ([(s_h2d, s_h2d.n), (s_sil, s_sil.n), (s_cv2, s_cv2.n),
                        (s_cv1, s_cv1.n), (s_mxt, s_mxt.n), (s_ctx, s_ctx.n)]
                       if (grp == 0 and hc < 2) else [])
            S.on("sync", f_h2r, waits=w_first)
            s_h2rp[pq].inc(16)

            def f_qa(g, grp=grp, hc=hc, pq=pq, qb=s_qb.n):
                if grp > 0 or hc >= 2:
                    g.wait_ge(s_qb.h, qb - 1)
                g.tensor_tensor(h2rawf[pq][:, :], h2raw[pq][:, :],
                                s2rep[:, grp * 512:(grp + 1) * 512], ALU.mult)
                g.drain()
                g.nop().then_inc(s_qa.h, 1)
            S.on("gpsimd", f_qa, waits=[(s_h2rp[pq], s_h2rp[pq].n), (s_s2d, s2rep_rdy)])
            s_qa.inc(1)

            def f_qb(v, grp=grp, hc=hc, pq=pq, ommn=s_omm.n):
                if grp > 0:
                    v.wait_ge(s_omm.h, ommn)
                v.tensor_scalar(h2q[hc][:, :], h2rawf[pq][:, :],
                                MAGIC, -MAGIC, ALU.add, ALU.add).then_inc(s_qb.h, 1)
            S.on("vector", f_qb, waits=[(s_qa, s_qa.n)])
            s_qb.inc(1)

        for ci in range(4):
            c = grp * 4 + ci
            bkp = (pb[4], pb[5]) if ci % 2 == 0 else (pb[6], pb[7])

            def f_omm(t, c=c, ci=ci, bkp=bkp, qbn=s_qb.n):
                t.wait_ge(s_qb.h, qbn - 16)
                mm = None
                for hc in range(NHC):
                    if hc == 16:
                        t.wait_ge(s_qb.h, qbn)
                    for half in range(2):
                        mm = nc.tensor.matmul(
                            bkp[half][:, :], h2q[hc][:, ci * 128:(ci + 1) * 128],
                            w2qsb[hc][:, half * 512:(half + 1) * 512],
                            start=(hc == 0), stop=(hc == NHC - 1))
                mm.then_inc(s_omm.h, 1)
            S.on("tensor", f_omm, waits=[(s_w2r, w2r_rdy),
                                         (s_oev, s_oev.n - 1 if c >= 2 else 0)])
            s_omm.inc(1)

            sxr = s_xa if c % 2 == 0 else s_xb

            def f_xat2(sy, c=c, sxr=sxr, oevn=s_oev.n):
                if c >= 2:
                    sy.wait_ge(s_oev.h, oevn - 1)
                sy.dma_start(out=xbuf[c % 2][:], in_=xattnd[c * 128:(c + 1) * 128, :]).then_inc(sxr.h, 16)
            S.on("sync", f_xat2, waits=[(s_xad, s_xad.n)] if c < 2 else [])
            sxr.inc(16)

            def f_oev(v, c=c, bkp=bkp, sxr=sxr, xr=sxr.n):
                v.wait_ge(sxr.h, xr)
                for half in range(2):
                    last = v.scalar_tensor_tensor(out_sb[:, half * 512:(half + 1) * 512],
                                                  bkp[half][:, :], statc[:, 64 + c:64 + c + 1],
                                                  xbuf[c % 2][:, half * 512:(half + 1) * 512],
                                                  ALU.mult, ALU.add)
                last.then_inc(s_oev.h, 1)
            S.on("vector", f_oev, waits=[(s_omm, s_omm.n), (s_out, s_out.n),
                                         (s_xad, s_xad.n)])
            s_oev.inc(1)

            def f_out(sy, c=c):
                sy.dma_start(out=y_d[c * 128:(c + 1) * 128, :], in_=out_sb[:, :]).then_inc(s_out.h, 16)
            S.on("sync", f_out, waits=[(s_oev, s_oev.n)])
            s_out.inc(16)

    def f_fin(sy):
        sy.wait_ge(s_out.h, s_out.n)
    S.on("sync", f_fin)

    S.build()
    return nc



def _bf16(a):
    return np.ascontiguousarray(a.astype(ml_dtypes.bfloat16))


def _make_consts():
    ident = np.eye(128, dtype=np.float32)
    oblk = np.zeros((128, 136), np.float32)
    oblk[0:64, 0] = NDIAG
    oblk[64:128, 1] = NDIAG
    oblk[0:64, 4 + 2] = NDIAG
    oblk[64:128, 4 + 3] = NDIAG
    for d in range(8):
        base = 8 + d * 16
        oblk[0:64, base + 2 * d] = NDIAG
        oblk[64:128, base + 2 * d + 1] = NDIAG
    onesrow = np.ones((1, 128), np.float32)
    return _bf16(ident), _bf16(oblk), _bf16(onesrow)


def kernel(L0_attn_x, L0_attn_wq, L0_attn_wk, L0_attn_wv, L0_attn_wo, L0_attn_bo,
           L0_attn_proj, L0_ffn_w1, L0_ffn_conv_w, L0_ffn_conv_b, L0_ffn_w2):
    x = np.ascontiguousarray(np.asarray(L0_attn_x, dtype=np.float32))
    wq = np.ascontiguousarray(np.asarray(L0_attn_wq, np.float32))
    wk = np.ascontiguousarray(np.asarray(L0_attn_wk, np.float32))
    wv = np.ascontiguousarray(np.asarray(L0_attn_wv, np.float32))
    wo = np.ascontiguousarray(np.asarray(L0_attn_wo, np.float32))
    bo = np.ascontiguousarray(np.asarray(L0_attn_bo, np.float32)).reshape(1, D)
    projT = np.ascontiguousarray(np.asarray(L0_attn_proj, np.float32).T)
    w1s = np.ascontiguousarray(
        np.asarray(L0_ffn_w1, np.float32).T.reshape(D, HID // 128, 128).transpose(1, 0, 2))
    w2t = np.ascontiguousarray(np.asarray(L0_ffn_w2, np.float32).T)
    cw = np.ascontiguousarray(
        np.asarray(L0_ffn_conv_w, np.float32).reshape(HID // 128, 128, 3)
        .transpose(1, 0, 2).reshape(128, -1))
    cb = np.ascontiguousarray(
        np.asarray(L0_ffn_conv_b, np.float32).reshape(HID // 128, 128).T)
    identbf, oblk, onesrow = _make_consts()

    nc1 = build_program("stab")
    in1 = []
    for c in range(B):
        in1.append({
            "x": np.ascontiguousarray(x[c]),
            "wk": wk, "projT": projT, "identbf": identbf, "oblk": oblk,
        })
    res1 = run_bass_kernel_spmd(nc1, in1, core_ids=list(range(B)))
    stab8 = np.stack([np.asarray(res1.results[c]["ystab"]).reshape(()) for c in range(B)]).reshape(1, B)
    stab8 = np.ascontiguousarray(stab8.astype(np.float32))

    nc2 = build_program("main")
    in2 = []
    for c in range(B):
        in2.append({
            "x": np.ascontiguousarray(x[c]),
            "wq": wq, "wk": wk, "wv": wv, "wo": wo, "bo": bo,
            "projT": projT, "identbf": identbf, "oblk": oblk,
            "onesrow": onesrow, "w1s": w1s, "w2t": w2t,
            "cw": cw, "cb": cb, "stab8": stab8,
        })
    res2 = run_bass_kernel_spmd(nc2, in2, core_ids=list(range(B)))
    out = np.stack([np.asarray(res2.results[c]["y"]) for c in range(B)], axis=0)
    return out.astype(np.float32)


if __name__ == "__main__":
    pass
